# version 02 (final)
# baseline (speedup 1.0000x reference)
import numpy as np
import ml_dtypes
from contextlib import ExitStack

import concourse.bass as bass
import concourse.mybir as mybir
from concourse.bass_utils import run_bass_kernel_spmd

F32 = mybir.dt.float32
BF16 = mybir.dt.bfloat16
I32 = mybir.dt.int32
AF = mybir.ActivationFunctionType
ALU = mybir.AluOpType
AXX = mybir.AxisListType.X
NPBF = ml_dtypes.bfloat16

ENGS = ("pe", "ve", "se", "po", "sy")
_ENG_ATTR = {"pe": "tensor", "ve": "vector", "se": "scalar", "po": "gpsimd", "sy": "sync"}


class Sched:

    NDMA = 8

    def __init__(self, nc, ctx):
        self.nc = nc
        self.q = {e: [] for e in ENGS}
        self.cnt = {e: 0 for e in ENGS}
        self.sem = {e: ctx.enter_context(nc.semaphore(f"tok_{e}")) for e in ENGS}
        for qn in ("sy", "po", "se"):
            for i in range(self.NDMA):
                self.sem[f"{qn}d{i}"] = ctx.enter_context(nc.semaphore(f"{qn}d{i}"))
        self.dma_uses = {}
        self.dma_next = {"sy": 0, "po": 0, "se": 0}
        self.last_wait = {e: {} for e in ENGS}
        self.res = {}

    def _deps_for(self, reads, writes):
        deps = []
        for k in reads:
            st = self.res.get(k)
            if st and st["w"]:
                deps.append(st["w"])
        for k in writes:
            st = self.res.get(k)
            if st:
                if st["w"]:
                    deps.append(st["w"])
                for e2, c2 in st["r"].items():
                    deps.append((e2, c2))
        return deps

    def _post(self, token, reads, writes):
        for k in reads:
            st = self.res.setdefault(k, {"w": None, "r": {}})
            e, c = token
            st["r"][e] = max(st["r"].get(e, 0), c)
        for k in writes:
            self.res[k] = {"w": token, "r": {}}

    def raw(self, eng, fn, deps=(), inc=1, dma_sem=None):
        waits = []
        lw = self.last_wait[eng]
        for (e2, v2) in deps:
            if v2 > lw.get(e2, 0):
                lw[e2] = v2
                waits.append((e2, v2))
        self.cnt[eng] += inc
        self.q[eng].append((waits, fn, inc, dma_sem))
        return (eng, self.cnt[eng])

    def op(self, eng, fn, r=(), w=(), extra_deps=(), inc=1):
        deps = self._deps_for(r, w) + list(extra_deps)
        tok = self.raw(eng, fn, deps, inc)
        self._post(tok, r, w)
        return tok

    def dma(self, eng, out_ap, in_ap, r=(), w=(), extra_deps=()):
        i = self.dma_next[eng]
        self.dma_next[eng] = (i + 1) % self.NDMA
        sname = f"{eng}d{i}"
        uses = self.dma_uses.get(sname, 0)
        deps = self._deps_for(r, w) + list(extra_deps)
        if uses:
            deps = deps + [(sname, 16 * uses)]
        self.dma_uses[sname] = uses + 1
        self.raw(eng, lambda e: e.dma_start(out=out_ap, in_=in_ap), deps, inc=0, dma_sem=(sname, 16))
        tok = (sname, 16 * (uses + 1))
        self._post(tok, r, w)
        return tok

    def mmgroup(self, fns, r=(), w=()):
        deps = self._deps_for(r, w)
        first = True
        for fn in fns[:-1]:
            self.raw("pe", fn, deps if first else (), inc=0)
            first = False
        tok = self.raw("pe", fns[-1], deps if first else (), inc=1)
        self._post(tok, r, w)
        return tok

    def build(self, block):
        deps = [(sn, 16 * n) for sn, n in self.dma_uses.items()]
        self.raw("po", lambda e: None, deps, inc=0)
        sem = self.sem

        def mk(eng):
            items = self.q[eng]

            def body(e):
                for (waits, fn, inc, dma_sem) in items:
                    for (e2, v2) in waits:
                        e.wait_ge(sem[e2], v2)
                    ins = fn(e)
                    if ins is None:
                        continue
                    if dma_sem is not None:
                        ins.then_inc(sem[dma_sem[0]], dma_sem[1])
                    elif inc:
                        ins.then_inc(sem[eng], inc)
            return body

        for eng in ENGS:
            if self.q[eng]:
                getattr(block, _ENG_ATTR[eng])(mk(eng))


def ceil_div(a, b):
    return (a + b - 1) // b



def emit_ln_transpose(sch, S, x_d, xnT_d, nln, D, extra_scale=None):
    nkc = D // 128
    for i in range(nln):
        b = i % 2
        xk = f"xin{b}"
        sch.dma("sy", S[xk][:, :], x_d[i * 128:(i + 1) * 128, :], r=(), w=(xk,))
        sch.op("ve", lambda e, b=b: e.reduce_sum(S["s1c"][:, :], S[f"xin{b}"][:, :], axis=AXX),
               r=(xk,), w=("s1c",))
        sch.op("se", lambda e, b=b: e.activation(S["sqs"][:, :], S[f"xin{b}"][:, :], AF.Square,
                                                 accum_out=S["s2c"][:, :]),
               r=(xk,), w=("sqs", "s2c"))
        sch.op("ve", lambda e: e.tensor_scalar(S["mcol"][:, :], S["s1c"][:, :], 1.0 / D, None, ALU.mult),
               r=("s1c",), w=("mcol",))
        sch.op("ve", lambda e: e.tensor_tensor(S["vcol"][:, :], S["mcol"][:, :], S["mcol"][:, :], ALU.mult),
               r=("mcol",), w=("vcol",))
        sch.op("ve", lambda e: e.scalar_tensor_tensor(S["vcol"][:, :], S["s2c"][:, :], 1.0 / D, S["vcol"][:, :], ALU.mult, ALU.subtract),
               r=("s2c", "vcol"), w=("vcol",))
        sch.op("ve", lambda e: e.tensor_scalar(S["vcol"][:, :], S["vcol"][:, :], 1e-5, None, ALU.add),
               r=("vcol",), w=("vcol",))
        sch.op("se", lambda e: e.activation(S["rcol"][:, :], S["vcol"][:, :], AF.Sqrt),
               r=("vcol",), w=("rcol",))
        sch.op("ve", lambda e: e.reciprocal(S["rcol"][:, :], S["rcol"][:, :]),
               r=("rcol",), w=("rcol",))
        xnk = f"xnst{b}"
        sch.op("ve", lambda e, b=b: e.tensor_scalar(S[f"xnst{b}"][:, :], S[f"xin{b}"][:, :],
                                                    S["mcol"][:, 0:1], S["rcol"][:, 0:1], ALU.subtract, ALU.mult),
               r=(xk, "mcol", "rcol"), w=(xnk,))
        for kc in range(nkc):
            pb = kc % 2
            pk = f"ptr{pb}"
            sch.op("pe", lambda e, b=b, kc=kc, pb=pb: e.transpose(
                S[f"ptr{pb}"][:, :],
                S[f"xnst{b}"][:, kc * 128:(kc + 1) * 128], S["identb"][:, :]),
                r=(xnk, "identb"), w=(pk,))
            sch.op("ve", lambda e, b=b, kc=kc, pb=pb: e.tensor_copy(
                S[f"xtst{b}"][:, kc * 128:(kc + 1) * 128], S[f"ptr{pb}"][:, :]),
                r=(pk,), w=(f"xtst{b}.{kc}",))
        sch.dma("po", xnT_d[:, :, i * 128:(i + 1) * 128].rearrange("kc p f -> p kc f"),
                S[f"xtst{b}"].ap().rearrange("p (kc f) -> p kc f", kc=nkc),
                r=tuple(f"xtst{b}.{kc}" for kc in range(nkc)), w=("xnT",))
    return "xnT"


def emit_gemm(sch, S, xnT_d, w_dram, n_kc, n_mc, tts, evac_fn, wname="wblk",
              cast_w=True, rhs_key="rhs", nbuf_banks=4):
    nblk = ceil_div(n_mc, 8)
    for blk in range(nblk):
        mcs = list(range(blk * 8, min(blk * 8 + 8, n_mc)))
        nmc = len(mcs)
        for kc in range(n_kc):
            sch.dma("po" if cast_w else "sy",
                    S[wname][:, kc * 1024: kc * 1024 + nmc * 128],
                    w_dram[kc * 128:(kc + 1) * 128, mcs[0] * 128: mcs[0] * 128 + nmc * 128],
                    r=(), w=(f"{wname}.{kc}",))
        for ti, (off, n) in enumerate(tts):
            b = ti % 2
            rk = f"{rhs_key}{b}"
            sch.dma("sy", S[rk].ap().rearrange("p (kc f) -> p kc f", kc=n_kc)[:, :, 0:n],
                    xnT_d[:, :, off:off + n].rearrange("kc p f -> p kc f"), r=("xnT",), w=(rk,))
            for mi, mc in enumerate(mcs):
                bank = mi % nbuf_banks
                bkey = f"pacc{bank}"
                fns = [
                    (lambda e, kc=kc, mi=mi, n=n, bank=bank, rk=rk: e.matmul(
                        S[f"pacc{bank}"][:, 0:n],
                        S[wname][:, kc * 1024 + mi * 128: kc * 1024 + mi * 128 + 128],
                        S[rk][:, kc * 512: kc * 512 + n],
                        start=(kc == 0), stop=(kc == n_kc - 1)))
                    for kc in range(n_kc)
                ]
                sch.mmgroup(fns, r=(rk,) + tuple(f"{wname}.{kc}" for kc in range(n_kc)),
                            w=(bkey,))
                evac_fn(mc, ti, (off, n), S[bkey], bkey)



def build_u1(cfg):
    D = 2048
    NT_OWN = cfg["nt_own"]
    NTOK = cfg["ntok"]
    NLN = NTOK // 128
    NOWN = NT_OWN // 512
    tts = [(i * 512, 512) for i in range(NTOK // 512)]
    if NTOK % 512:
        tts.append(((NTOK // 512) * 512, NTOK % 512))
    NSEL_G = cfg["nsel_g"]
    NSEL_D = cfg["nsel_d"]

    nc = bass.Bass("TRN2", target_bir_lowering=False, debug=False)
    dt = nc.dram_tensor
    x_d = dt("x", [NTOK, D], F32, kind="ExternalInput").ap()
    wqkv_d = dt("w_qkv", [D, 8192], F32, kind="ExternalInput").ap()
    wz_d = dt("w_z", [D, 4096], F32, kind="ExternalInput").ap()
    wb_d = dt("w_b", [D, 32], F32, kind="ExternalInput").ap()
    wa_d = dt("w_a", [D, 32], F32, kind="ExternalInput").ap()
    cw_d = dt("conv_w", [8192, 4], F32, kind="ExternalInput").ap()
    dtb_d = dt("dt_bias", [32, 1], F32, kind="ExternalInput").ap()
    alog_d = dt("A_log", [32, 1], F32, kind="ExternalInput").ap()
    lut_d = dt("lut", [1, 16], F32, kind="ExternalInput").ap()
    gidx_d = dt("gidx", [NSEL_G, 128, D], I32, kind="ExternalInput").ap()
    uidx_d = dt("uidx", [NSEL_G, 128, D], I32, kind="ExternalInput").ap()
    didx_d = dt("didx", [NSEL_D, 128, 5504], I32, kind="ExternalInput").ap()
    identb_d = dt("identb", [128, 128], BF16, kind="ExternalInput").ap()
    onescol_d = dt("onescol", [128, 1], F32, kind="ExternalInput").ap()
    onesrow_d = dt("onesrow", [1, 128], F32, kind="ExternalInput").ap()

    xnT_d = dt("xnT", [16, 128, NTOK], BF16, kind="ExternalOutput").ap()
    q_d = dt("q", [16, 128, NT_OWN], BF16, kind="ExternalOutput").ap()
    k_d = dt("k", [16, 128, NT_OWN], BF16, kind="ExternalOutput").ap()
    v_d = dt("v", [32, 128, NT_OWN], BF16, kind="ExternalOutput").ap()
    zs_d = dt("zs", [32, 128, NTOK], BF16, kind="ExternalOutput").ap()
    gv_d = dt("gv", [32, NTOK], F32, kind="ExternalOutput").ap()
    bv_d = dt("bv", [32, NTOK], F32, kind="ExternalOutput").ap()
    wgt_d = dt("wgts", [NSEL_G, 16, 128, 128], BF16, kind="ExternalOutput").ap()
    wut_d = dt("wuts", [NSEL_G, 16, 128, 128], BF16, kind="ExternalOutput").ap()
    wdt_d = dt("wdts", [NSEL_D, 43, 128, 128], BF16, kind="ExternalOutput").ap()

    with ExitStack() as ctx:
        E = ctx.enter_context
        sch = Sched(nc, ctx)
        S = {}

        def sb(name, shape, dtype):
            S[name] = E(nc.sbuf_tensor("s_" + name, shape, dtype))
            return S[name]

        def ps(name, shape, dtype=F32):
            S[name] = E(nc.psum_tensor("p_" + name, shape, dtype))
            return S[name]

        sb("identb", [128, 128], BF16)
        sb("onescol", [128, 1], F32)
        sb("onesrow", [1, 128], F32)
        sb("lutrow", [1, 16], F32)
        sb("lutb", [128, 16], F32)
        sb("dtb", [32, 1], F32)
        sb("nega", [32, 1], F32)
        for i in range(2):
            sb(f"xin{i}", [128, D], F32)
            sb(f"xnst{i}", [128, D], BF16)
            sb(f"xtst{i}", [128, D], BF16)
            sb(f"rhs{i}", [128, 16 * 512], BF16)
        sb("sqs", [128, D], F32)
        for n in ("s1c", "s2c", "mcol", "vcol", "rcol"):
            sb(n, [128, 1], F32)
        sb("wblk", [128, 16 * 1024], BF16)
        sb("mxA", [128, 8 * 512], F32)
        sb("mxB", [128, 8 * 512], F32)
        sb("win", [128, 515], F32)
        sb("cvt", [128, 512], F32)
        sb("cwcol", [128, 4 * 8], F32)
        sb("sqt", [128, 512], F32)
        sb("rsrow", [1, 512], F32)
        sb("outbf", [128, 512], BF16)
        sb("bcolt", [32, 512], F32)
        sb("spa", [32, 512], F32)
        sb("spb", [32, 512], F32)
        sb("sidx", [128, 5504], I32)
        sb("sidxb", [128, 5504], BF16)
        sb("selt", [128, 5504], BF16)
        sb("selacc", [128, 5504], BF16)
        sb("seltr", [128, 128], BF16)
        for i in range(4):
            ps(f"pacc{i}", [128, 512], F32)
        for i in range(2):
            ps(f"ptr{i}", [128, 128], BF16)
        ps("pmisc", [128, 512], F32)

        block = E(nc.Block())

        sch.dma("sy", S["identb"][:, :], identb_d, w=("identb",))
        sch.dma("sy", S["onescol"][:, :], onescol_d, w=("onescol",))
        sch.dma("sy", S["onesrow"][:, :], onesrow_d, w=("onesrow",))
        sch.dma("sy", S["lutrow"][:, :], lut_d, w=("lutrow",))
        sch.dma("sy", S["dtb"][:, :], dtb_d, w=("dtb",))
        sch.dma("sy", S["nega"][:, :], alog_d, w=("nega",))
        sch.op("pe", lambda e: e.matmul(S["pmisc"][:, 0:16], S["onesrow"][0:1, :], S["lutrow"][0:1, :],
                                        start=True, stop=True), r=("onesrow", "lutrow"), w=("pmisc",))
        sch.op("ve", lambda e: e.tensor_copy(S["lutb"][:, :], S["pmisc"][:, 0:16]), r=("pmisc",), w=("lutb",))
        sch.op("se", lambda e: e.activation(S["nega"][:, :], S["nega"][:, :], AF.Exp), r=("nega",), w=("nega",))
        sch.op("ve", lambda e: e.tensor_scalar(S["nega"][:, :], S["nega"][:, :], -1.0, None, ALU.mult),
               r=("nega",), w=("nega",))

        emit_ln_transpose(sch, S, x_d, xnT_d, NLN, D)

        conv_cw = set()

        def evac_qkv(mc, ti, offn, psum, bkey):
            off, n = offn
            mi = mc % 8
            sl = slice(mi * 512, mi * 512 + 512)
            if mc not in conv_cw:
                conv_cw.add(mc)
                sch.dma("sy", S["cwcol"][:, mi * 4:(mi + 1) * 4], cw_d[mc * 128:(mc + 1) * 128, :],
                        w=(f"cw{mi}",))
            cur = "mxA" if ti % 2 == 0 else "mxB"
            prev = "mxB" if ti % 2 == 0 else "mxA"
            ck = f"{cur}.{mi}"
            pk = f"{prev}.{mi}"
            sch.op("se", lambda e, sl=sl, n=n, cur=cur: e.activation(S[cur][:, sl][:, 0:n], psum[:, 0:n], AF.Identity),
                   r=(bkey,), w=(ck,))
            if 1 <= ti and (ti - 1) < NOWN:
                j = ti - 1
                sch.op("ve", lambda e, sl=sl, prev=prev: e.tensor_copy(S["win"][:, 0:512], S[prev][:, sl]),
                       r=(pk,), w=("win",))
                sch.op("ve", lambda e, sl=sl, cur=cur: e.tensor_copy(S["win"][:, 512:515], S[cur][:, sl][:, 0:3]),
                       r=(ck, "win"), w=("win",))
                for jj in range(4):
                    if jj == 0:
                        sch.op("ve", lambda e, mi=mi: e.tensor_scalar(
                            S["cvt"][:, :], S["win"][:, 0:512], S["cwcol"][:, mi * 4: mi * 4 + 1], None, ALU.mult),
                            r=("win", f"cw{mi}"), w=("cvt",))
                    else:
                        sch.op("ve", lambda e, jj=jj, mi=mi: e.scalar_tensor_tensor(
                            S["cvt"][:, :], S["win"][:, jj:jj + 512], S["cwcol"][:, mi * 4 + jj: mi * 4 + jj + 1],
                            S["cvt"][:, :], ALU.mult, ALU.add),
                            r=("win", f"cw{mi}", "cvt"), w=("cvt",))
                sch.op("se", lambda e: e.activation(S["cvt"][:, :], S["cvt"][:, :], AF.Silu), r=("cvt",), w=("cvt",))
                if mc < 32:
                    sch.op("se", lambda e: e.activation(S["sqt"][:, :], S["cvt"][:, :], AF.Square), r=("cvt",), w=("sqt",))
                    sch.op("pe", lambda e: e.matmul(S["pmisc"][0:1, :], S["onescol"][:, 0:1], S["sqt"][:, :],
                                                    start=True, stop=True), r=("sqt", "onescol"), w=("pmisc",))
                    sch.op("ve", lambda e: e.tensor_scalar(S["rsrow"][:, :], S["pmisc"][0:1, 0:512], 1e-6, None, ALU.add),
                           r=("pmisc",), w=("rsrow",))
                    sch.op("se", lambda e: e.activation(S["rsrow"][:, :], S["rsrow"][:, :], AF.Sqrt), r=("rsrow",), w=("rsrow",))
                    sch.op("ve", lambda e: e.reciprocal(S["rsrow"][:, :], S["rsrow"][:, :]), r=("rsrow",), w=("rsrow",))
                    if mc < 16:
                        sch.op("ve", lambda e: e.tensor_scalar(S["rsrow"][:, :], S["rsrow"][:, :], 128.0 ** -0.5, None, ALU.mult),
                               r=("rsrow",), w=("rsrow",))
                    sch.op("pe", lambda e: e.matmul(S["pmisc"][:, 0:512], S["onesrow"][0:1, :], S["rsrow"][0:1, :],
                                                    start=True, stop=True), r=("rsrow", "onesrow"), w=("pmisc",))
                    sch.op("ve", lambda e: e.tensor_tensor(S["outbf"][:, :], S["cvt"][:, :], S["pmisc"][:, 0:512], ALU.mult),
                           r=("cvt", "pmisc"), w=("outbf",))
                    dst = (q_d if mc < 16 else k_d)[mc % 16, :, j * 512:(j + 1) * 512]
                    sch.dma("po", dst, S["outbf"][:, :], r=("outbf",), w=("OUT",))
                else:
                    sch.op("ve", lambda e: e.tensor_copy(S["outbf"][:, :], S["cvt"][:, :]), r=("cvt",), w=("outbf",))
                    sch.dma("po", v_d[mc - 32, :, j * 512:(j + 1) * 512], S["outbf"][:, :], r=("outbf",), w=("OUT",))

        emit_gemm(sch, S, xnT_d, wqkv_d, 16, 64, tts, evac_qkv)

        def evac_z(mc, ti, offn, psum, bkey):
            off, n = offn
            sch.op("se", lambda e, n=n: e.activation(S["outbf"][:, 0:n], psum[:, 0:n], AF.Silu),
                   r=(bkey,), w=("outbf",))
            sch.dma("po", zs_d[mc, :, off:off + n], S["outbf"][:, 0:n], r=("outbf",), w=("OUT",))

        emit_gemm(sch, S, xnT_d, wz_d, 16, 32, tts, evac_z)

        for kc in range(16):
            sch.dma("po", S["wblk"][:, kc * 1024: kc * 1024 + 32], wb_d[kc * 128:(kc + 1) * 128, :],
                    w=(f"wblk.{kc}",))
            sch.dma("po", S["wblk"][:, kc * 1024 + 32: kc * 1024 + 64], wa_d[kc * 128:(kc + 1) * 128, :],
                    r=(f"wblk.{kc}",), w=(f"wblk.{kc}b",))
        for ti, (off, n) in enumerate(tts):
            b = ti % 2
            rk = f"rhs{b}"
            sch.dma("sy", S[rk].ap().rearrange("p (kc f) -> p kc f", kc=16)[:, :, 0:n],
                    xnT_d[:, :, off:off + n].rearrange("kc p f -> p kc f"), r=("xnT",), w=(rk,))
            fns = [(lambda e, kc=kc, n=n, rk=rk: e.matmul(
                S["pacc0"][0:64, 0:n], S["wblk"][:, kc * 1024: kc * 1024 + 64],
                S[rk][:, kc * 512: kc * 512 + n], start=(kc == 0), stop=(kc == 15)))
                for kc in range(16)]
            sch.mmgroup(fns, r=(rk,) + tuple(f"wblk.{kc}" for kc in range(16)) + tuple(f"wblk.{kc}b" for kc in range(16)),
                        w=("pacc0",))
            sch.op("se", lambda e, n=n: e.activation(S["bcolt"][:, 0:n], S["pacc0"][0:32, 0:n], AF.Sigmoid),
                   r=("pacc0",), w=("bcolt",))
            sch.dma("po", bv_d[:, off:off + n], S["bcolt"][:, 0:n], r=("bcolt",), w=("OUT",))
            sch.op("se", lambda e, n=n: e.activation(S["spa"][:, 0:n], S["pacc0"][32:64, 0:n], AF.Abs, bias=S["dtb"][:, 0:1]),
                   r=("pacc0", "dtb"), w=("spa",))
            sch.op("se", lambda e, n=n: e.activation(S["spa"][:, 0:n], S["spa"][:, 0:n], AF.Exp, scale=-1.0),
                   r=("spa",), w=("spa",))
            sch.op("se", lambda e, n=n: e.activation(S["spa"][:, 0:n], S["spa"][:, 0:n], AF.Ln, bias=1.0),
                   r=("spa",), w=("spa",))
            sch.op("se", lambda e, n=n: e.activation(S["spb"][:, 0:n], S["pacc0"][32:64, 0:n], AF.Relu, bias=S["dtb"][:, 0:1]),
                   r=("pacc0", "dtb"), w=("spb",))
            sch.op("ve", lambda e, n=n: e.tensor_tensor(S["spa"][:, 0:n], S["spa"][:, 0:n], S["spb"][:, 0:n], ALU.add),
                   r=("spa", "spb"), w=("spa",))
            sch.op("ve", lambda e, n=n: e.tensor_scalar(S["spa"][:, 0:n], S["spa"][:, 0:n], S["nega"][:, 0:1], None, ALU.mult),
                   r=("spa", "nega"), w=("spa",))
            sch.dma("po", gv_d[:, off:off + n], S["spa"][:, 0:n], r=("spa",), w=("OUT",))

        def select_block(idx_dram, nt, out_dram, ncols):
            for it in range(nt):
                sch.dma("sy", S["sidx"][:, 0:ncols], idx_dram[it, :, :], w=("sidx",))
                sch.op("ve", lambda e, ncols=ncols: e.tensor_copy(S["sidxb"][:, 0:ncols], S["sidx"][:, 0:ncols]),
                       r=("sidx",), w=("sidxb",))
                sch.op("ve", lambda e, ncols=ncols: e.tensor_scalar(
                    S["selacc"][:, 0:ncols], S["sidxb"][:, 0:ncols], 0.0, S["lutb"][:, 0:1], ALU.is_equal, ALU.mult),
                    r=("sidxb", "lutb"), w=("selacc",))
                for j in range(1, 16):
                    sch.op("ve", lambda e, j=j, ncols=ncols: e.tensor_scalar(
                        S["selt"][:, 0:ncols], S["sidxb"][:, 0:ncols], float(j), S["lutb"][:, j:j + 1], ALU.is_equal, ALU.mult),
                        r=("sidxb", "lutb"), w=("selt",))
                    sch.op("ve", lambda e, ncols=ncols: e.tensor_tensor(
                        S["selacc"][:, 0:ncols], S["selacc"][:, 0:ncols], S["selt"][:, 0:ncols], ALU.add),
                        r=("selacc", "selt"), w=("selacc",))
                for kc in range(ncols // 128):
                    pb = kc % 2
                    sch.op("pe", lambda e, kc=kc, pb=pb: e.transpose(
                        S[f"ptr{pb}"][:, :], S["selacc"][:, kc * 128:(kc + 1) * 128], S["identb"][:, :]),
                        r=("selacc", "identb"), w=(f"ptr{pb}",))
                    sch.op("ve", lambda e, pb=pb: e.tensor_copy(S["seltr"][:, :], S[f"ptr{pb}"][:, :]),
                           r=(f"ptr{pb}",), w=("seltr",))
                    sch.dma("po", out_dram[it, kc, :, :], S["seltr"][:, :], r=("seltr",), w=("OUT",))

        select_block(gidx_d, NSEL_G, wgt_d, D)
        select_block(uidx_d, NSEL_G, wut_d, D)
        select_block(didx_d, NSEL_D, wdt_d, 5504)

        sch.build(block)
    return nc



def build_u2(cfg):
    NU = cfg["nu"]
    T = cfg["t_seq"]
    NCH = T // 128
    NKH = NU // 2

    nc = bass.Bass("TRN2", target_bir_lowering=False, debug=False)
    dt = nc.dram_tensor
    q_d = dt("q", [NKH, 128, T], BF16, kind="ExternalInput").ap()
    k_d = dt("k", [NKH, 128, T], BF16, kind="ExternalInput").ap()
    v_d = dt("v", [NU, 128, T], BF16, kind="ExternalInput").ap()
    gc_d = dt("gcols", [NU, 128, NCH], F32, kind="ExternalInput").ap()
    bc_d = dt("bcols", [NU, 128, NCH], F32, kind="ExternalInput").ap()
    nw_d = dt("nwrow", [1, 128], F32, kind="ExternalInput").ap()
    identb_d = dt("identb", [128, 128], BF16, kind="ExternalInput").ap()
    identf_d = dt("identf", [128, 128], F32, kind="ExternalInput").ap()
    triu_d = dt("triu", [128, 128], F32, kind="ExternalInput").ap()
    mup_d = dt("mup", [128, 128], F32, kind="ExternalInput").ap()
    mlow_d = dt("mlow", [128, 128], F32, kind="ExternalInput").ap()
    ones_d = dt("ones128", [128, 128], F32, kind="ExternalInput").ap()
    onesrow_d = dt("onesrow", [1, 128], F32, kind="ExternalInput").ap()
    e127_d = dt("e127", [128, 1], F32, kind="ExternalInput").ap()
    o_d = dt("o", [NU, T, 128], BF16, kind="ExternalOutput").ap()

    with ExitStack() as ctx:
        E = ctx.enter_context
        sch = Sched(nc, ctx)
        S = {}

        def sb(name, shape, dtype):
            S[name] = E(nc.sbuf_tensor("s_" + name, shape, dtype))
            return S[name]

        def ps(name, shape, dtype=F32):
            S[name] = E(nc.psum_tensor("p_" + name, shape, dtype))
            return S[name]

        for n, sh, dtp in [
            ("identb", [128, 128], BF16), ("identf", [128, 128], F32),
            ("triu", [128, 128], F32), ("mup", [128, 128], F32), ("mlow", [128, 128], F32),
            ("ones", [128, 128], F32), ("onesrow", [1, 128], F32), ("e127", [128, 1], F32),
            ("nwrow", [1, 128], F32), ("nwb", [128, 128], F32),
            ("squ", [128, T], BF16), ("sku", [128, T], BF16), ("svu", [128, T], BF16),
            ("gcol", [128, NCH], F32), ("bcol", [128, NCH], F32),
            ("Gc", [128, NCH], F32), ("eG", [128, NCH], F32), ("bEG", [128, NCH], F32),
            ("eGCb", [128, NCH], F32),
            ("Gd", [128, 128], F32), ("tmsk", [128, 128], F32),
            ("Dincl", [128, 128], F32), ("DinclT", [128, 128], F32),
            ("ATb", [128, 128], BF16), ("tE", [128, 128], F32), ("tbc", [128, 128], F32),
            ("Fb", [128, 128], BF16), ("Ftb", [128, 128], BF16),
            ("Xa", [128, 128], BF16), ("XTa", [128, 128], BF16),
            ("Xb2", [128, 128], BF16), ("XTb2", [128, 128], BF16),
            ("Pa", [128, 128], BF16), ("PTa", [128, 128], BF16),
            ("Pb", [128, 128], BF16), ("PTb", [128, 128], BF16),
            ("Rv", [128, 128], F32), ("Rb", [128, 128], BF16),
            ("KtTb", [128, 128], BF16), ("Db", [128, 128], BF16),
            ("Oin", [128, 128], F32), ("Otm", [128, 128], F32),
            ("rs1", [128, 1], F32), ("rstd", [128, 1], F32),
            ("outb", [128, 128], BF16),
            ("Sst", [128, 128], F32), ("Sbf", [128, 128], BF16),
        ]:
            sb(n, sh, dtp)
        ps("pb0", [128, 128]); ps("pb1", [128, 128]); ps("pb2", [128, 128])
        ps("pb3", [128, 128]); ps("pb4", [128, 128]); ps("pb6", [128, 128])
        ps("pb5", [128, 128], BF16)
        ps("pb7", [128, 128])

        block = E(nc.Block())

        for nm, ap in [("identb", identb_d), ("identf", identf_d), ("triu", triu_d),
                       ("mup", mup_d), ("mlow", mlow_d), ("ones", ones_d),
                       ("onesrow", onesrow_d), ("e127", e127_d), ("nwrow", nw_d)]:
            sch.dma("sy", S[nm][:, :], ap, w=(nm,))
        sch.op("pe", lambda e: e.matmul(S["pb7"][:, :], S["onesrow"][0:1, :], S["nwrow"][0:1, :],
                                        start=True, stop=True), r=("onesrow", "nwrow"), w=("pb7",))
        sch.op("ve", lambda e: e.tensor_copy(S["nwb"][:, :], S["pb7"][:, :]), r=("pb7",), w=("nwb",))

        for u in range(NU):
            kh = u // 2
            sch.dma("sy", S["squ"][:, :], q_d[kh, :, :], w=("squ",))
            sch.dma("sy", S["sku"][:, :], k_d[kh, :, :], w=("sku",))
            sch.dma("sy", S["svu"][:, :], v_d[u, :, :], w=("svu",))
            sch.dma("sy", S["gcol"][:, :], gc_d[u, :, :], w=("gcol",))
            sch.dma("sy", S["bcol"][:, :], bc_d[u, :, :], w=("bcol",))
            sch.op("pe", lambda e: e.matmul(S["pb0"][:, 0:NCH], S["triu"][:, :], S["gcol"][:, :],
                                            start=True, stop=True), r=("triu", "gcol"), w=("pb0",))
            sch.op("ve", lambda e: e.tensor_copy(S["Gc"][:, :], S["pb0"][:, 0:NCH]), r=("pb0",), w=("Gc",))
            sch.op("se", lambda e: e.activation(S["eG"][:, :], S["Gc"][:, :], AF.Exp), r=("Gc",), w=("eG",))
            sch.op("ve", lambda e: e.scalar_tensor_tensor(S["bEG"][:, :], S["bcol"][:, :], -1.0, S["eG"][:, :],
                                                          ALU.mult, ALU.mult), r=("bcol", "eG"), w=("bEG",))
            sch.op("pe", lambda e: e.matmul(S["pb0"][0:1, 0:NCH], S["e127"][:, 0:1], S["eG"][:, :],
                                            start=True, stop=True), r=("e127", "eG"), w=("pb0",))
            sch.op("ve", lambda e: e.tensor_copy(S["eGCb"][0:1, :], S["pb0"][0:1, 0:NCH]),
                   r=("pb0",), w=("eGCrow",))
            sch.op("pe", lambda e: e.matmul(S["pb0"][:, 0:NCH], S["onesrow"][0:1, :], S["eGCb"][0:1, :],
                                            start=True, stop=True), r=("eGCrow", "onesrow"), w=("pb0",))
            sch.op("ve", lambda e: e.tensor_copy(S["eGCb"][:, :], S["pb0"][:, 0:NCH]), r=("pb0",), w=("eGCb", "eGCrow"))
            sch.op("ve", lambda e: e.memset(S["Sst"][:, :], 0.0), r=(), w=("Sst",))
            sch.op("ve", lambda e: e.memset(S["Sbf"][:, :], 0.0), r=(), w=("Sbf",))

            for c in range(NCH):
                cs = slice(c * 128, (c + 1) * 128)
                qk_ = ("squ", "sku", "svu")
                sch.op("ve", lambda e, c=c: e.tensor_scalar(S["tbc"][:, :], S["ones"][:, :], S["Gc"][:, c:c + 1],
                                                            None, ALU.mult), r=("ones", "Gc"), w=("tbc",))
                sch.op("pe", lambda e: e.matmul(S["pb0"][:, :], S["tbc"][:, :], S["identf"][:, :],
                                                start=True, stop=True), r=("tbc", "identf"), w=("pb0",))
                sch.op("ve", lambda e, c=c: e.tensor_scalar(S["Gd"][:, :], S["pb0"][:, :], S["Gc"][:, c:c + 1],
                                                            None, ALU.subtract), r=("pb0", "Gc"), w=("Gd",))
                sch.op("ve", lambda e: e.tensor_tensor(S["tmsk"][:, :], S["Gd"][:, :], S["mup"][:, :], ALU.add),
                       r=("Gd", "mup"), w=("tmsk",))
                sch.op("se", lambda e: e.activation(S["Dincl"][:, :], S["tmsk"][:, :], AF.Exp),
                       r=("tmsk",), w=("Dincl",))
                sch.op("ve", lambda e: e.scalar_tensor_tensor(S["tmsk"][:, :], S["Gd"][:, :], -1.0, S["mlow"][:, :],
                                                              ALU.mult, ALU.add), r=("Gd", "mlow"), w=("tmsk",))
                sch.op("se", lambda e: e.activation(S["DinclT"][:, :], S["tmsk"][:, :], AF.Exp),
                       r=("tmsk",), w=("DinclT",))
                sch.op("pe", lambda e, cs=cs: e.matmul(S["pb1"][:, :], S["sku"][:, cs], S["sku"][:, cs],
                                                       start=True, stop=True), r=("sku",), w=("pb1",))
                sch.op("pe", lambda e, cs=cs: e.matmul(S["pb2"][:, :], S["sku"][:, cs], S["squ"][:, cs],
                                                       start=True, stop=True), r=("sku", "squ"), w=("pb2",))
                sch.op("ve", lambda e: e.tensor_tensor(S["ATb"][:, :], S["pb2"][:, :], S["Dincl"][:, :], ALU.mult),
                       r=("pb2", "Dincl"), w=("ATb",))
                sch.op("ve", lambda e: e.tensor_tensor(S["tE"][:, :], S["pb1"][:, :], S["Dincl"][:, :], ALU.mult),
                       r=("pb1", "Dincl"), w=("tE",))
                sch.op("ve", lambda e: e.tensor_tensor(S["tE"][:, :], S["tE"][:, :], S["identf"][:, :], ALU.subtract),
                       r=("tE", "identf"), w=("tE",))
                sch.op("ve", lambda e, c=c: e.tensor_scalar(S["tbc"][:, :], S["ones"][:, :], S["bcol"][:, c:c + 1],
                                                            None, ALU.mult), r=("ones", "bcol"), w=("tbc",))
                sch.op("pe", lambda e: e.matmul(S["pb0"][:, :], S["tbc"][:, :], S["identf"][:, :],
                                                start=True, stop=True), r=("tbc", "identf"), w=("pb0",))
                sch.op("ve", lambda e: e.tensor_tensor(S["Fb"][:, :], S["tE"][:, :], S["pb0"][:, :], ALU.mult),
                       r=("tE", "pb0"), w=("Fb",))
                sch.op("ve", lambda e: e.tensor_tensor(S["tE"][:, :], S["pb1"][:, :], S["DinclT"][:, :], ALU.mult),
                       r=("pb1", "DinclT"), w=("tE",))
                sch.op("ve", lambda e: e.tensor_tensor(S["tE"][:, :], S["tE"][:, :], S["identf"][:, :], ALU.subtract),
                       r=("tE", "identf"), w=("tE",))
                sch.op("ve", lambda e, c=c: e.tensor_scalar(S["Ftb"][:, :], S["tE"][:, :], S["bcol"][:, c:c + 1],
                                                            None, ALU.mult), r=("tE", "bcol"), w=("Ftb",))
                sch.op("ve", lambda e: e.scalar_tensor_tensor(S["Xa"][:, :], S["Fb"][:, :], -1.0, S["identb"][:, :],
                                                              ALU.mult, ALU.add), r=("Fb", "identb"), w=("Xa",))
                sch.op("ve", lambda e: e.scalar_tensor_tensor(S["XTa"][:, :], S["Ftb"][:, :], -1.0, S["identb"][:, :],
                                                              ALU.mult, ALU.add), r=("Ftb", "identb"), w=("XTa",))
                P, PT, X, XT = "Fb", "Ftb", "Xa", "XTa"
                Pn, PTn, Xn, XTn = "Pa", "PTa", "Xb2", "XTb2"
                nstage = 6
                for s2 in range(nstage):
                    last = (s2 == nstage - 1)
                    sch.op("pe", lambda e, P=P, PT=PT: e.matmul(S["pb3"][:, :], S[PT][:, :], S[P][:, :],
                                                                start=True, stop=True), r=(P, PT), w=("pb3",))
                    sch.op("se", lambda e, Pn=Pn: e.activation(S[Pn][:, :], S["pb3"][:, :], AF.Identity),
                           r=("pb3",), w=(Pn,))
                    if not last:
                        sch.op("pe", lambda e, P=P, PT=PT: e.matmul(S["pb4"][:, :], S[P][:, :], S[PT][:, :],
                                                                    start=True, stop=True), r=(P, PT), w=("pb4",))
                        sch.op("se", lambda e, PTn=PTn: e.activation(S[PTn][:, :], S["pb4"][:, :], AF.Identity),
                               r=("pb4",), w=(PTn,))
                    sch.op("pe", lambda e, XT=XT, Pn=Pn: e.matmul(S["pb3"][:, :], S[XT][:, :], S[Pn][:, :],
                                                                  start=True, stop=True), r=(XT, Pn), w=("pb3",))
                    sch.op("ve", lambda e, X=X, Xn=Xn: e.tensor_tensor(S[Xn][:, :], S["pb3"][:, :], S[X][:, :], ALU.add),
                           r=("pb3", X), w=(Xn,))
                    if not last:
                        sch.op("pe", lambda e, XT=XT, Pn=Pn: e.matmul(S["pb4"][:, :], S[Pn][:, :], S[XT][:, :],
                                                                      start=True, stop=True), r=(Pn, XT), w=("pb4",))
                        sch.op("ve", lambda e, XT=XT, XTn=XTn: e.tensor_tensor(S[XTn][:, :], S["pb4"][:, :], S[XT][:, :], ALU.add),
                               r=("pb4", XT), w=(XTn,))
                    P, Pn = Pn, ("Pb" if Pn == "Pa" else "Pa")
                    PT, PTn = PTn, ("PTb" if PTn == "PTa" else "PTa")
                    X, Xn = Xn, ("Xa" if Xn == "Xb2" else "Xb2")
                    XT, XTn = XTn, ("XTa" if XTn == "XTb2" else "XTb2")
                Xfin = X
                sch.op("pe", lambda e, cs=cs: e.transpose(S["pb5"][:, :], S["svu"][:, cs], S["identb"][:, :]),
                       r=("svu", "identb"), w=("pb5",))
                sch.op("ve", lambda e, c=c: e.tensor_scalar(S["Rv"][:, :], S["pb5"][:, :], S["bcol"][:, c:c + 1],
                                                            None, ALU.mult), r=("pb5", "bcol"), w=("Rv",))
                sch.op("pe", lambda e, cs=cs: e.transpose(S["pb5"][:, :], S["sku"][:, cs], S["identb"][:, :]),
                       r=("sku", "identb", "Rv"), w=("pb5",))
                sch.op("ve", lambda e: e.tensor_scalar(S["KtTb"][:, :], S["pb5"][:, :], S["Dincl"][:, 127:128],
                                                       None, ALU.mult), r=("pb5", "Dincl"), w=("KtTb",))
                sch.op("pe", lambda e, cs=cs: e.matmul(S["pb6"][:, :], S["sku"][:, cs], S["Sbf"][:, :],
                                                       start=True, stop=True), r=("sku", "Sbf"), w=("pb6",))
                sch.op("ve", lambda e, c=c: e.scalar_tensor_tensor(S["Rb"][:, :], S["pb6"][:, :], S["bEG"][:, c:c + 1],
                                                                   S["Rv"][:, :], ALU.mult, ALU.add),
                       r=("pb6", "bEG", "Rv"), w=("Rb",))
                sch.op("pe", lambda e, Xfin=Xfin: e.matmul(S["pb6"][:, :], S[Xfin][:, :], S["Rb"][:, :],
                                                           start=True, stop=True), r=(Xfin, "Rb"), w=("pb6",))
                sch.op("se", lambda e: e.activation(S["Db"][:, :], S["pb6"][:, :], AF.Identity), r=("pb6",), w=("Db",))
                sch.op("pe", lambda e, cs=cs: e.matmul(S["pb0"][:, :], S["squ"][:, cs], S["Sbf"][:, :],
                                                       start=True, stop=True), r=("squ", "Sbf"), w=("pb0",))
                sch.op("pe", lambda e: e.matmul(S["pb2"][:, :], S["ATb"][:, :], S["Db"][:, :],
                                                start=True, stop=True), r=("ATb", "Db"), w=("pb2",))
                sch.op("se", lambda e: e.activation(S["Oin"][:, :], S["pb2"][:, :], AF.Identity), r=("pb2",), w=("Oin",))
                sch.op("ve", lambda e, c=c: e.scalar_tensor_tensor(S["Otm"][:, :], S["pb0"][:, :], S["eG"][:, c:c + 1],
                                                                   S["Oin"][:, :], ALU.mult, ALU.add),
                       r=("pb0", "eG", "Oin"), w=("Otm",))
                sch.op("se", lambda e: e.activation(S["tE"][:, :], S["Otm"][:, :], AF.Square, accum_out=S["rs1"][:, :]),
                       r=("Otm",), w=("tE", "rs1"))
                sch.op("ve", lambda e: e.tensor_scalar(S["rstd"][:, :], S["rs1"][:, :], 1.0 / 128, 1e-6, ALU.mult, ALU.add),
                       r=("rs1",), w=("rstd",))
                sch.op("se", lambda e: e.activation(S["rstd"][:, :], S["rstd"][:, :], AF.Sqrt), r=("rstd",), w=("rstd",))
                sch.op("ve", lambda e: e.reciprocal(S["rstd"][:, :], S["rstd"][:, :]), r=("rstd",), w=("rstd",))
                sch.op("ve", lambda e: e.tensor_scalar(S["Otm"][:, :], S["Otm"][:, :], S["rstd"][:, 0:1], None, ALU.mult),
                       r=("Otm", "rstd"), w=("Otm",))
                sch.op("ve", lambda e: e.tensor_tensor(S["outb"][:, :], S["Otm"][:, :], S["nwb"][:, :], ALU.mult),
                       r=("Otm", "nwb"), w=("outb",))
                sch.dma("po", o_d[u, c * 128:(c + 1) * 128, :], S["outb"][:, :], r=("outb",), w=("OUT",))
                sch.op("pe", lambda e: e.matmul(S["pb6"][:, :], S["KtTb"][:, :], S["Db"][:, :],
                                                start=True, stop=True), r=("KtTb", "Db"), w=("pb6",))
                sch.op("ve", lambda e, c=c: e.scalar_tensor_tensor(S["Sst"][:, :], S["Sst"][:, :], S["eGCb"][:, c:c + 1],
                                                                   S["pb6"][:, :], ALU.mult, ALU.add),
                       r=("Sst", "eGCb", "pb6"), w=("Sst",))
                sch.op("ve", lambda e: e.tensor_copy(S["Sbf"][:, :], S["Sst"][:, :]), r=("Sst",), w=("Sbf",))

        sch.build(block)
    return nc


def emit_gemm2(sch, S, act_dram, act_res, w_dram, n_kc, n_mc, tts, evac_fn,
               mblk=8, cast_w=True, kcw=None):
    wq = "po" if cast_w else "sy"
    for blk in range(ceil_div(n_mc, mblk)):
        mcs = list(range(blk * mblk, min(blk * mblk + mblk, n_mc)))
        nmc = len(mcs)
        wstep = mblk * 128
        for kc in range(n_kc):
            sch.dma(wq, S["wblk"][:, kc * wstep: kc * wstep + nmc * 128],
                    w_dram[kc * 128:(kc + 1) * 128, mcs[0] * 128: mcs[0] * 128 + nmc * 128],
                    r=(), w=(f"wblk.{kc}",))
        for ti, (off, n) in enumerate(tts):
            b = ti % 2
            rk = f"rhs{b}"
            sch.dma("sy", S[rk][:, 0:n_kc * 512].rearrange("p (kc f) -> p kc f", kc=n_kc)[:, :, 0:n],
                    act_dram[:, :, off:off + n].rearrange("kc p f -> p kc f"),
                    r=(act_res,), w=(rk,))
            for mi, mc in enumerate(mcs):
                bank = mi % 4
                bkey = f"pacc{bank}"
                fns = [
                    (lambda e, kc=kc, mi=mi, n=n, bank=bank, rk=rk, wstep=wstep: e.matmul(
                        S[f"pacc{bank}"][:, 0:n],
                        S["wblk"][:, kc * wstep + mi * 128: kc * wstep + mi * 128 + 128],
                        S[rk][:, kc * 512: kc * 512 + n],
                        start=(kc == 0), stop=(kc == n_kc - 1)))
                    for kc in range(n_kc)
                ]
                sch.mmgroup(fns, r=(rk,) + tuple(f"wblk.{kc}" for kc in range(n_kc)),
                            w=(bkey,))
                evac_fn(mc, ti, (off, n), S[bkey], bkey)



def build_u3(cfg):
    D = 2048
    NT_OWN = cfg["nt_own"]
    NTOK = cfg["ntok"]
    PREF = 3
    NTT = NT_OWN // 128
    tts = [(i * 512, 512) for i in range(NT_OWN // 512)]

    nc = bass.Bass("TRN2", target_bir_lowering=False, debug=False)
    dt = nc.dram_tensor
    x_d = dt("x", [NT_OWN, D], F32, kind="ExternalInput").ap()
    oin_d = dt("oin", [32, NT_OWN // 128, 128, 128], BF16, kind="ExternalInput").ap()
    zs_d = dt("zs", [32, 128, NTOK], BF16, kind="ExternalInput").ap()
    xnT_d = dt("xnT", [16, 128, NTOK], BF16, kind="ExternalInput").ap()
    wgt_d = dt("wgt", [D, 5504], BF16, kind="ExternalInput").ap()
    wut_d = dt("wut", [D, 5504], BF16, kind="ExternalInput").ap()
    wdt_d = dt("wdt", [5504, D], BF16, kind="ExternalInput").ap()
    wout_d = dt("w_out", [4096, D], F32, kind="ExternalInput").ap()
    identb_d = dt("identb", [128, 128], BF16, kind="ExternalInput").ap()
    identf_d = dt("identf", [128, 128], F32, kind="ExternalInput").ap()
    x1_d = dt("x1", [NT_OWN, D], F32, kind="ExternalOutput").ap()
    gated_d = dt("gated", [32, 128, NT_OWN], BF16).ap()
    sfm_d = dt("sfm", [16, 128, NT_OWN], F32).ap()
    sg_d = dt("sgf", [43, 128, NT_OWN], BF16).ap()
    hfm_d = dt("hfm", [43, 128, NT_OWN], BF16).ap()
    gfm_d = dt("gfm", [16, 128, NT_OWN], F32).ap()

    with ExitStack() as ctx:
        E = ctx.enter_context
        sch = Sched(nc, ctx)
        S = {}

        def sb(name, shape, dtype):
            S[name] = E(nc.sbuf_tensor("s_" + name, shape, dtype))
            return S[name]

        def ps(name, shape, dtype=F32):
            S[name] = E(nc.psum_tensor("p_" + name, shape, dtype))
            return S[name]

        sb("identb", [128, 128], BF16)
        sb("identf", [128, 128], F32)
        sb("ovh", [128, NT_OWN], BF16)
        sb("zvh", [128, NTOK], BF16)
        sb("gatedst", [128, NT_OWN], BF16)
        sb("wblk", [128, 32768], BF16)
        for i in range(2):
            sb(f"rhs{i}", [128, 43 * 512], BF16)
        sb("sstage", [128, 512], F32)
        sb("sgst", [128, 512], BF16)
        sb("fst", [128, 512], F32)
        sb("xtile", [128, D], F32)
        sb("x1tile", [128, D], F32)
        sb("sgtile", [128, 128], BF16)
        for i in range(4):
            ps(f"pacc{i}", [128, 512])
        ps("ptrb", [128, 128], BF16)
        ps("ptrf", [128, 128])

        block = E(nc.Block())
        sch.dma("sy", S["identb"][:, :], identb_d, w=("identb",))
        sch.dma("sy", S["identf"][:, :], identf_d, w=("identf",))

        for vh in range(32):
            sch.dma("sy", S["ovh"].ap().rearrange("p (t f) -> p t f", t=NTT),
                    oin_d[vh].rearrange("t p f -> p t f"), w=("ovh",))
            sch.dma("sy", S["zvh"][:, :], zs_d[vh], w=("zvh",))
            for tt in range(NTT):
                sch.op("pe", lambda e, tt=tt: e.transpose(S["ptrb"][:, :], S["ovh"][:, tt * 128:(tt + 1) * 128],
                                                          S["identb"][:, :]), r=("ovh", "identb"), w=("ptrb",))
                sch.op("ve", lambda e, tt=tt: e.tensor_tensor(
                    S["gatedst"][:, tt * 128:(tt + 1) * 128], S["ptrb"][:, :],
                    S["zvh"][:, PREF + tt * 128: PREF + (tt + 1) * 128], ALU.mult),
                    r=("ptrb", "zvh"), w=(f"gst.{tt}",))
            sch.dma("po", gated_d[vh], S["gatedst"][:, :],
                    r=tuple(f"gst.{tt}" for tt in range(NTT)), w=("gated",))

        def evac_s(mc, ti, offn, psum, bkey):
            off, n = offn
            sch.op("se", lambda e, n=n: e.activation(S["sstage"][:, 0:n], psum[:, 0:n], AF.Identity),
                   r=(bkey,), w=("sstage",))
            sch.dma("po", sfm_d[mc, :, off:off + n], S["sstage"][:, 0:n], r=("sstage",), w=("sfm",))

        emit_gemm2(sch, S, gated_d, "gated", wout_d, 32, 16, tts, evac_s, mblk=8, cast_w=True)

        def evac_gate(mc, ti, offn, psum, bkey):
            off, n = offn
            sch.op("se", lambda e, n=n: e.activation(S["sgst"][:, 0:n], psum[:, 0:n], AF.Silu),
                   r=(bkey,), w=("sgst",))
            sch.dma("po", sg_d[mc, :, off:off + n], S["sgst"][:, 0:n], r=("sgst",), w=("sg",))

        emit_gemm2(sch, S, xnT_d[:, :, PREF:PREF + NT_OWN], "XN", wgt_d, 16, 43, tts, evac_gate, mblk=8, cast_w=False)

        def evac_up(mc, ti, offn, psum, bkey):
            off, n = offn
            sch.dma("sy", S["sgst"][:, 0:n], sg_d[mc, :, off:off + n], r=("sg",), w=("sgst",))
            sch.op("ve", lambda e, n=n: e.tensor_tensor(S["sgst"][:, 0:n], psum[:, 0:n], S["sgst"][:, 0:n], ALU.mult),
                   r=(bkey, "sgst"), w=("sgst",))
            sch.dma("po", hfm_d[mc, :, off:off + n], S["sgst"][:, 0:n], r=("sgst",), w=("hfm",))

        emit_gemm2(sch, S, xnT_d[:, :, PREF:PREF + NT_OWN], "XN", wut_d, 16, 43, tts, evac_up, mblk=8, cast_w=False)

        def evac_down(mc, ti, offn, psum, bkey):
            off, n = offn
            sch.op("se", lambda e, n=n: e.activation(S["fst"][:, 0:n], psum[:, 0:n], AF.Silu),
                   r=(bkey,), w=("fst",))
            sch.dma("sy", S["sstage"][:, 0:n], sfm_d[mc, :, off:off + n], r=("sfm",), w=("sstage",))
            sch.op("ve", lambda e, n=n: e.tensor_tensor(S["fst"][:, 0:n], S["fst"][:, 0:n], S["sstage"][:, 0:n], ALU.mult),
                   r=("fst", "sstage"), w=("fst",))
            sch.dma("po", gfm_d[mc, :, off:off + n], S["fst"][:, 0:n], r=("fst",), w=("gfm",))

        emit_gemm2(sch, S, hfm_d, "hfm", wdt_d, 43, 16, tts, evac_down, mblk=4, cast_w=False)

        for tt in range(NTT):
            sch.dma("sy", S["xtile"][:, :], x_d[tt * 128:(tt + 1) * 128, :], w=("xtile",))
            for dm in range(16):
                sch.dma("sy", S["fst"][:, 0:128], gfm_d[dm, :, tt * 128:(tt + 1) * 128], r=("gfm",), w=("fst",))
                sch.op("pe", lambda e, dm=dm: e.transpose(S["ptrf"][:, :], S["fst"][:, 0:128], S["identf"][:, :]),
                       r=("fst", "identf"), w=("ptrf",))
                sch.op("ve", lambda e, dm=dm: e.tensor_tensor(
                    S["x1tile"][:, dm * 128:(dm + 1) * 128], S["ptrf"][:, :],
                    S["xtile"][:, dm * 128:(dm + 1) * 128], ALU.add),
                    r=("ptrf", "xtile"), w=(f"x1t.{dm}",))
            sch.dma("po", x1_d[tt * 128:(tt + 1) * 128, :], S["x1tile"][:, :],
                    r=tuple(f"x1t.{dm}" for dm in range(16)), w=("OUT",))

        sch.build(block)
    return nc



def build_u4(cfg):
    D = 2048
    NT = cfg["nt_own"]
    NTT = NT // 128
    NLN = NTT
    tts = [(i * 512, 512) for i in range(NT // 512)]

    nc = bass.Bass("TRN2", target_bir_lowering=False, debug=False)
    dt = nc.dram_tensor
    x1_d = dt("x1", [NT, D], F32, kind="ExternalInput").ap()
    wqT_d = dt("wqT", [D, D], F32, kind="ExternalInput").ap()
    wkT_d = dt("wkT", [D, D], F32, kind="ExternalInput").ap()
    wvT_d = dt("wvT", [D, D], F32, kind="ExternalInput").ap()
    bq_d = dt("bq", [128, 16], F32, kind="ExternalInput").ap()
    bk_d = dt("bk", [128, 16], F32, kind="ExternalInput").ap()
    bv_d = dt("bvrow", [1, D], F32, kind="ExternalInput").ap()
    wmq_d = dt("wmapqT", [128, 64], F32, kind="ExternalInput").ap()
    wmk_d = dt("wmapkT", [128, 64], F32, kind="ExternalInput").ap()
    identb_d = dt("identb", [128, 128], BF16, kind="ExternalInput").ap()
    onesrow_d = dt("onesrow", [1, 128], F32, kind="ExternalInput").ap()
    xn1T_d = dt("xn1T", [16, 128, NT], BF16, kind="ExternalOutput").ap()
    q2_d = dt("q2", [16, 64, NT], BF16, kind="ExternalOutput").ap()
    kvp_d = dt("kvp", [16, 64, 129], F32, kind="ExternalOutput").ap()
    qfm_d = dt("qfm", [16, 128, NT], BF16).ap()
    kfm_d = dt("kfm", [16, 128, NT], BF16).ap()
    vtm_d = dt("vtm", [NT, D], BF16).ap()

    with ExitStack() as ctx:
        E = ctx.enter_context
        sch = Sched(nc, ctx)
        S = {}

        def sb(name, shape, dtype):
            S[name] = E(nc.sbuf_tensor("s_" + name, shape, dtype))
            return S[name]

        def ps(name, shape, dtype=F32):
            S[name] = E(nc.psum_tensor("p_" + name, shape, dtype))
            return S[name]

        sb("identb", [128, 128], BF16)
        sb("onesrow", [1, 128], F32)
        for i in range(2):
            sb(f"xin{i}", [128, D], F32)
            sb(f"xnst{i}", [128, D], BF16)
            sb(f"xtst{i}", [128, D], BF16)
            sb(f"rhs{i}", [128, 16 * 512], BF16)
        sb("sqs", [128, D], F32)
        for n in ("s1c", "s2c", "mcol", "vcol", "rcol"):
            sb(n, [128, 1], F32)
        sb("wblk", [128, 16 * 2048], BF16)
        sb("bqc", [128, 16], F32)
        sb("bkc", [128, 16], F32)
        sb("bvrow", [1, D], F32)
        sb("bvb", [128, D], F32)
        sb("wmq", [128, 64], BF16)
        sb("wmk", [128, 64], BF16)
        sb("stg", [128, 512], F32)
        sb("stg2", [128, 512], F32)
        sb("outbf", [128, 512], BF16)
        sb("qh", [128, NT], BF16)
        sb("kh", [128, NT], BF16)
        sb("k2t", [128, 64], BF16)
        sb("vaug", [128, 129], BF16)
        sb("kvst", [64, 129], F32)
        for i in range(4):
            ps(f"pacc{i}", [128, 512])
        ps("ptr0", [128, 128], BF16)
        ps("ptr1", [128, 128], BF16)
        ps("pkv", [64, 129])
        ps("pk2", [128, 64])

        block = E(nc.Block())
        sch.dma("sy", S["identb"][:, :], identb_d, w=("identb",))
        sch.dma("sy", S["onesrow"][:, :], onesrow_d, w=("onesrow",))
        sch.dma("sy", S["bqc"][:, :], bq_d, w=("bqc",))
        sch.dma("sy", S["bkc"][:, :], bk_d, w=("bkc",))
        sch.dma("sy", S["bvrow"][:, :], bv_d, w=("bvrow",))
        sch.dma("po", S["wmq"][:, :], wmq_d, w=("wmq",))
        sch.dma("po", S["wmk"][:, :], wmk_d, w=("wmk",))
        for i in range(4):
            sch.op("pe", lambda e, i=i: e.matmul(S["pacc0"][:, 0:512], S["onesrow"][0:1, :],
                                                 S["bvrow"][0:1, i * 512:(i + 1) * 512], start=True, stop=True),
                   r=("onesrow", "bvrow"), w=("pacc0",))
            sch.op("ve", lambda e, i=i: e.tensor_copy(S["bvb"][:, i * 512:(i + 1) * 512], S["pacc0"][:, 0:512]),
                   r=("pacc0",), w=("bvb",))

        emit_ln_transpose(sch, S, x1_d, xn1T_d, NLN, D)

        def mk_evac(dst_d, bias_key):
            def evac(mc, ti, offn, psum, bkey):
                off, n = offn
                sch.op("ve", lambda e, n=n, mc=mc: e.tensor_scalar(
                    S["outbf"][:, 0:n], psum[:, 0:n], S[bias_key][:, mc:mc + 1], None, ALU.add),
                    r=(bkey, bias_key), w=("outbf",))
                sch.dma("po", dst_d[mc, :, off:off + n], S["outbf"][:, 0:n], r=("outbf",), w=(dst_d.tensor.name,))
            return evac

        emit_gemm2(sch, S, xn1T_d, "xnT", wqT_d, 16, 16, tts, mk_evac(qfm_d, "bqc"), mblk=8, cast_w=True)
        emit_gemm2(sch, S, xn1T_d, "xnT", wkT_d, 16, 16, tts, mk_evac(kfm_d, "bkc"), mblk=8, cast_w=True)

        for kc in range(16):
            sch.dma("po", S["wblk"][:, kc * 2048:(kc + 1) * 2048], wvT_d[kc * 128:(kc + 1) * 128, :],
                    w=(f"wblk.{kc}",))
        for tt in range(NTT):
            b = tt % 2
            rk = f"rhs{b}"
            sch.dma("sy", S[rk][:, 0:16 * 128].rearrange("p (kc f) -> p kc f", kc=16),
                    xn1T_d[:, :, tt * 128:(tt + 1) * 128].rearrange("kc p f -> p kc f"),
                    r=("xnT",), w=(rk,))
            for dhc in range(4):
                bank = dhc % 2
                fns = [(lambda e, kc=kc, dhc=dhc, bank=bank, rk=rk: e.matmul(
                    S[f"pacc{bank}"][:, 0:512],
                    S[rk][:, kc * 128:(kc + 1) * 128],
                    S["wblk"][:, kc * 2048 + dhc * 512: kc * 2048 + dhc * 512 + 512],
                    start=(kc == 0), stop=(kc == 15))) for kc in range(16)]
                sch.mmgroup(fns, r=(rk,) + tuple(f"wblk.{kc}" for kc in range(16)), w=(f"pacc{bank}",))
                sch.op("ve", lambda e, dhc=dhc, bank=bank: e.tensor_tensor(
                    S["outbf"][:, 0:512], S[f"pacc{bank}"][:, 0:512], S["bvb"][:, dhc * 512:(dhc + 1) * 512], ALU.add),
                    r=(f"pacc{bank}", "bvb"), w=("outbf",))
                sch.dma("po", vtm_d[tt * 128:(tt + 1) * 128, dhc * 512:(dhc + 1) * 512], S["outbf"][:, 0:512],
                        r=("outbf",), w=("vtm",))

        for h in range(16):
            sch.dma("sy", S["qh"][:, :], qfm_d[h], r=("qfm",), w=("qh",))
            for ti, (off, n) in enumerate(tts):
                sch.op("pe", lambda e, off=off, n=n: e.matmul(S["pacc2"][0:64, 0:n], S["wmq"][:, :],
                                                              S["qh"][:, off:off + n], start=True, stop=True),
                       r=("wmq", "qh"), w=("pacc2",))
                sch.op("se", lambda e, n=n: e.activation(S["stg"][0:64, 0:n], S["pacc2"][0:64, 0:n], AF.Relu),
                       r=("pacc2",), w=("stg",))
                sch.op("se", lambda e, n=n: e.activation(S["stg2"][0:64, 0:n], S["pacc2"][0:64, 0:n], AF.Relu, scale=-1.0),
                       r=("pacc2",), w=("stg2",))
                sch.op("se", lambda e, n=n: e.activation(S["stg2"][0:64, 0:n], S["stg2"][0:64, 0:n], AF.Exp, scale=-1.0),
                       r=("stg2",), w=("stg2",))
                sch.op("ve", lambda e, n=n: e.tensor_tensor(S["outbf"][0:64, 0:n], S["stg"][0:64, 0:n],
                                                            S["stg2"][0:64, 0:n], ALU.add),
                       r=("stg", "stg2"), w=("outbf",))
                sch.dma("po", q2_d[h, :, off:off + n], S["outbf"][0:64, 0:n], r=("outbf",), w=("q2",))

        for h in range(16):
            sch.dma("sy", S["kh"][:, :], kfm_d[h], r=("kfm",), w=("kh",))
            for tt in range(NTT):
                sch.op("pe", lambda e, tt=tt: e.matmul(S["pk2"][:, :], S["kh"][:, tt * 128:(tt + 1) * 128],
                                                       S["wmk"][:, :], start=True, stop=True),
                       r=("kh", "wmk"), w=("pk2",))
                sch.op("se", lambda e: e.activation(S["stg"][:, 0:64], S["pk2"][:, :], AF.Relu),
                       r=("pk2",), w=("stg",))
                sch.op("se", lambda e: e.activation(S["stg2"][:, 0:64], S["pk2"][:, :], AF.Relu, scale=-1.0),
                       r=("pk2",), w=("stg2",))
                sch.op("se", lambda e: e.activation(S["stg2"][:, 0:64], S["stg2"][:, 0:64], AF.Exp, scale=-1.0),
                       r=("stg2",), w=("stg2",))
                sch.op("ve", lambda e: e.tensor_tensor(S["k2t"][:, :], S["stg"][:, 0:64], S["stg2"][:, 0:64], ALU.add),
                       r=("stg", "stg2"), w=("k2t",))
                sch.dma("sy", S["vaug"][:, 0:128], vtm_d[tt * 128:(tt + 1) * 128, h * 128:(h + 1) * 128],
                        r=("vtm",), w=("vaug",))
                sch.op("ve", lambda e: e.memset(S["vaug"][:, 128:129], 1.0), r=("vaug",), w=("vaug1",))
                sch.op("pe", lambda e, tt=tt: e.matmul(S["pkv"][:, :], S["k2t"][:, :], S["vaug"][:, :],
                                                       start=(tt == 0), stop=(tt == NTT - 1)),
                       r=("k2t", "vaug", "vaug1"), w=() if tt < NTT - 1 else ("pkv",),
                       extra_deps=sch._deps_for((), ("pkv",)) if tt == 0 else ())
            sch.op("ve", lambda e: e.tensor_copy(S["kvst"][:, :], S["pkv"][:, :]), r=("pkv",), w=("kvst",))
            sch.dma("po", kvp_d[h], S["kvst"][:, :], r=("kvst",), w=("OUT",))

        sch.build(block)
    return nc



def build_u5(cfg):
    D = 2048
    NT = cfg["nt_own"]
    NTT = NT // 128
    H5 = 5120
    tts = [(i * 512, 512) for i in range(NT // 512)]

    nc = bass.Bass("TRN2", target_bir_lowering=False, debug=False)
    dt = nc.dram_tensor
    x1_d = dt("x1", [NT, D], F32, kind="ExternalInput").ap()
    xn1T_d = dt("xn1T", [16, 128, NT], BF16, kind="ExternalInput").ap()
    q2_d = dt("q2", [16, 64, NT], BF16, kind="ExternalInput").ap()
    kva_d = dt("kvpa", [16, 64, 129], F32, kind="ExternalInput").ap()
    kvb_d = dt("kvpb", [16, 64, 129], F32, kind="ExternalInput").ap()
    woT_d = dt("woT", [D, D], F32, kind="ExternalInput").ap()
    bo_d = dt("bo", [128, 16], F32, kind="ExternalInput").ap()
    wfcT_d = dt("wfcT", [D, H5], F32, kind="ExternalInput").ap()
    wpjT_d = dt("wpjT", [H5, D], F32, kind="ExternalInput").ap()
    identb_d = dt("identb", [128, 128], BF16, kind="ExternalInput").ap()
    identf_d = dt("identf", [128, 128], F32, kind="ExternalInput").ap()
    onesrow_d = dt("onesrow", [1, 128], F32, kind="ExternalInput").ap()
    onescol_d = dt("onescol", [128, 1], F32, kind="ExternalInput").ap()
    x2_d = dt("x2", [NT, D], F32, kind="ExternalOutput").ap()
    kdbg = "ExternalOutput" if cfg.get("dbg") else "Internal"
    wfcq_d = dt("wfcq", [D, H5], BF16, kind=kdbg).ap()
    wpjq_d = dt("wpjq", [H5, D], BF16, kind=kdbg).ap()
    attn_d = dt("attnfm", [16, 128, NT], BF16, kind=kdbg).ap()
    s1_d = dt("s1f", [16, 128, NT], F32, kind=kdbg).ap()
    h1_d = dt("h1f", [40, 128, NT], BF16, kind=kdbg).ap()
    sg1_d = dt("sg1f", [16, 128, NT], F32, kind=kdbg).ap()

    with ExitStack() as ctx:
        E = ctx.enter_context
        sch = Sched(nc, ctx)
        S = {}

        def sb(name, shape, dtype):
            S[name] = E(nc.sbuf_tensor("s_" + name, shape, dtype))
            return S[name]

        def ps(name, shape, dtype=F32):
            S[name] = E(nc.psum_tensor("p_" + name, shape, dtype))
            return S[name]

        sb("identb", [128, 128], BF16)
        sb("identf", [128, 128], F32)
        sb("onesrow", [1, 128], F32)
        sb("onescol", [128, 1], F32)
        sb("boc", [128, 16], F32)
        for i in range(2):
            sb(f"rhs{i}", [128, 40 * 512], BF16)
        sb("wblk", [128, 20480], BF16)
        sb("wtile", [128, 2560], F32)
        sb("sgn", [128, 2560], F32)
        sb("asum", [128, 1], F32)
        sb("asum2", [128, 1], F32)
        sb("amc", [128, 1], F32)
        sb("thrc", [128, 1], F32)
        sb("amc2", [128, 1], F32)
        sb("thrc2", [128, 1], F32)
        sb("scal", [1, 1], F32)
        sb("qh", [64, NT], BF16)
        sb("kvs", [64, 16 * 129], F32)
        sb("kvs2", [64, 16 * 129], F32)
        sb("kvbf", [64, 16 * 129], BF16)
        sb("rcol", [128, 1], F32)
        sb("attst", [128, 128], BF16)
        sb("attstT", [128, 128], BF16)
        sb("stg", [128, 512], F32)
        sb("stgb", [128, 512], BF16)
        sb("xtile", [128, D], F32)
        sb("x1tile", [128, D], F32)
        for i in range(4):
            ps(f"pacc{i}", [128, 512])
        ps("ptrb", [128, 128], BF16)
        ps("ptrf", [128, 128])
        ps("pmisc", [128, 512])

        block = E(nc.Block())
        sch.dma("sy", S["identb"][:, :], identb_d, w=("identb",))
        sch.dma("sy", S["identf"][:, :], identf_d, w=("identf",))
        sch.dma("sy", S["onesrow"][:, :], onesrow_d, w=("onesrow",))
        sch.dma("sy", S["onescol"][:, :], onescol_d, w=("onescol",))
        sch.dma("sy", S["boc"][:, :], bo_d, w=("boc",))

        def ternarize(w_dram, nrow_tiles, ncols, out_dram, amkey, thrkey, sumkey):
            csz = 2560
            nch = ceil_div(ncols, csz)
            chunks = [(i * csz, min(csz, ncols - i * csz)) for i in range(nch)]
            sch.op("ve", lambda e: e.memset(S[sumkey][:, :], 0.0), w=(sumkey,))
            for it in range(nrow_tiles):
              for (co, cn) in chunks:
                sch.dma("sy", S["wtile"][:, 0:cn], w_dram[it * 128:(it + 1) * 128, co:co + cn], w=("wtile",))
                sch.op("ve", lambda e, cn=cn: e.reduce_sum(S["asum2"][:, :], S["wtile"][:, 0:cn],
                                                           axis=AXX, apply_absolute_value=True),
                       r=("wtile",), w=("asum2",))
                sch.op("ve", lambda e: e.tensor_tensor(S[sumkey][:, :], S[sumkey][:, :], S["asum2"][:, :], ALU.add),
                       r=(sumkey, "asum2"), w=(sumkey,))
            sch.op("pe", lambda e: e.matmul(S["pmisc"][0:1, 0:1], S[sumkey][:, 0:1], S["onescol"][:, 0:1],
                                            start=True, stop=True), r=(sumkey, "onescol"), w=("pmisc",))
            sch.op("ve", lambda e, ncols=ncols, nrow_tiles=nrow_tiles: e.tensor_scalar(
                S["scal"][:, :], S["pmisc"][0:1, 0:1], 1.0 / (ncols * nrow_tiles * 128), 1e-5, ALU.mult, ALU.max),
                r=("pmisc",), w=("scal",))
            sch.op("pe", lambda e: e.matmul(S["pmisc"][:, 0:1], S["onesrow"][0:1, :], S["scal"][0:1, 0:1],
                                            start=True, stop=True), r=("scal", "onesrow"), w=("pmisc",))
            sch.op("ve", lambda e: e.tensor_copy(S[amkey][:, :], S["pmisc"][:, 0:1]), r=("pmisc",), w=(amkey,))
            sch.op("ve", lambda e: e.tensor_scalar(S[thrkey][:, :], S[amkey][:, :], 0.7, None, ALU.mult),
                   r=(amkey,), w=(thrkey,))
            for it in range(nrow_tiles):
              for (co, cn) in chunks:
                sch.dma("sy", S["wtile"][:, 0:cn], w_dram[it * 128:(it + 1) * 128, co:co + cn], w=("wtile",))
                sch.op("se", lambda e, cn=cn: e.activation(S["sgn"][:, 0:cn], S["wtile"][:, 0:cn], AF.Sign),
                       r=("wtile",), w=("sgn",))
                sch.op("se", lambda e, cn=cn: e.activation(S["wtile"][:, 0:cn], S["wtile"][:, 0:cn], AF.Abs),
                       r=("wtile",), w=("wtile",))
                sch.op("ve", lambda e, cn=cn: e.tensor_scalar(S["wtile"][:, 0:cn], S["wtile"][:, 0:cn],
                                                              S[thrkey][:, 0:1], None, ALU.is_gt),
                       r=("wtile", thrkey), w=("wtile",))
                sch.op("ve", lambda e, cn=cn: e.tensor_tensor(S["wtile"][:, 0:cn], S["wtile"][:, 0:cn],
                                                              S["sgn"][:, 0:cn], ALU.mult),
                       r=("wtile", "sgn"), w=("wtile",))
                sch.op("ve", lambda e, cn=cn: e.tensor_scalar(S["sgn"][:, 0:cn], S["wtile"][:, 0:cn],
                                                              S[amkey][:, 0:1], None, ALU.mult),
                       r=("wtile", amkey), w=("sgn",))
                sch.dma("po", out_dram[it * 128:(it + 1) * 128, co:co + cn], S["sgn"][:, 0:cn],
                        r=("sgn",), w=("WQ",))

        ternarize(wfcT_d, 16, H5, wfcq_d, "amc", "thrc", "asum")
        ternarize(wpjT_d, 40, D, wpjq_d, "amc2", "thrc2", "asum")

        sch.dma("sy", S["kvs"].ap().rearrange("p (h f) -> p h f", h=16),
                kva_d.rearrange("h p f -> p h f"), w=("kvs",))
        sch.dma("sy", S["kvs2"].ap().rearrange("p (h f) -> p h f", h=16),
                kvb_d.rearrange("h p f -> p h f"), w=("kvs2",))
        sch.op("ve", lambda e: e.tensor_tensor(S["kvs"][:, :], S["kvs"][:, :], S["kvs2"][:, :], ALU.add),
               r=("kvs", "kvs2"), w=("kvs",))
        sch.op("ve", lambda e: e.tensor_copy(S["kvbf"][:, :], S["kvs"][:, :]), r=("kvs",), w=("kvbf",))
        for h in range(16):
            sch.dma("sy", S["qh"][:, :], q2_d[h], r=(), w=("qh",))
            for tt in range(NTT):
                sch.op("pe", lambda e, tt=tt, h=h: e.matmul(S["pmisc"][:, 0:129], S["qh"][:, tt * 128:(tt + 1) * 128],
                                                            S["kvbf"][:, h * 129:(h + 1) * 129], start=True, stop=True),
                       r=("qh", "kvbf"), w=("pmisc",))
                sch.op("ve", lambda e: e.tensor_scalar(S["rcol"][:, :], S["pmisc"][:, 128:129], 1e-6, None, ALU.add),
                       r=("pmisc",), w=("rcol",))
                sch.op("ve", lambda e: e.reciprocal(S["rcol"][:, :], S["rcol"][:, :]), r=("rcol",), w=("rcol",))
                sch.op("ve", lambda e: e.tensor_scalar(S["attst"][:, :], S["pmisc"][:, 0:128], S["rcol"][:, 0:1],
                                                       None, ALU.mult), r=("pmisc", "rcol"), w=("attst",))
                sch.op("pe", lambda e: e.transpose(S["ptrb"][:, :], S["attst"][:, :], S["identb"][:, :]),
                       r=("attst", "identb"), w=("ptrb",))
                sch.op("ve", lambda e: e.tensor_copy(S["attstT"][:, :], S["ptrb"][:, :]), r=("ptrb",), w=("attstT",))
                sch.dma("po", attn_d[h, :, tt * 128:(tt + 1) * 128], S["attstT"][:, :], r=("attstT",), w=("attn",))

        def evac_s1(mc, ti, offn, psum, bkey):
            off, n = offn
            sch.op("ve", lambda e, n=n, mc=mc: e.tensor_scalar(S["stg"][:, 0:n], psum[:, 0:n],
                                                               S["boc"][:, mc:mc + 1], None, ALU.add),
                   r=(bkey, "boc"), w=("stg",))
            sch.dma("po", s1_d[mc, :, off:off + n], S["stg"][:, 0:n], r=("stg",), w=("s1",))

        emit_gemm2(sch, S, attn_d, "attn", woT_d, 16, 16, tts, evac_s1, mblk=8, cast_w=True)

        def evac_fc(mc, ti, offn, psum, bkey):
            off, n = offn
            sch.op("se", lambda e, n=n: e.activation(S["stg"][:, 0:n], psum[:, 0:n], AF.Relu),
                   r=(bkey,), w=("stg",))
            sch.op("ve", lambda e, n=n: e.tensor_tensor(S["stgb"][:, 0:n], S["stg"][:, 0:n], S["stg"][:, 0:n], ALU.mult),
                   r=("stg",), w=("stgb",))
            sch.dma("po", h1_d[mc, :, off:off + n], S["stgb"][:, 0:n], r=("stgb",), w=("h1",))

        emit_gemm2(sch, S, xn1T_d, "xn1T", wfcq_d, 16, 40, tts, evac_fc, mblk=8, cast_w=False)

        def evac_pj(mc, ti, offn, psum, bkey):
            off, n = offn
            sch.op("se", lambda e, n=n: e.activation(S["stg"][:, 0:n], psum[:, 0:n], AF.Silu),
                   r=(bkey,), w=("stg",))
            sch.dma("sy", S["xtile"][:, 0:n],
                    s1_d[mc, :, off:off + n], r=("s1",), w=("xtile",))
            sch.op("ve", lambda e, n=n: e.tensor_tensor(S["stg"][:, 0:n], S["stg"][:, 0:n], S["xtile"][:, 0:n], ALU.mult),
                   r=("stg", "xtile"), w=("stg",))
            sch.dma("po", sg1_d[mc, :, off:off + n], S["stg"][:, 0:n], r=("stg",), w=("sg1",))

        emit_gemm2(sch, S, h1_d, "h1", wpjq_d, 40, 16, tts, evac_pj, mblk=4, cast_w=False)

        for tt in range(NTT):
            sch.dma("sy", S["xtile"][:, :], x1_d[tt * 128:(tt + 1) * 128, :], r=("xtile",), w=("xtile",))
            for dm in range(16):
                sch.dma("sy", S["stg"][:, 0:128], sg1_d[dm, :, tt * 128:(tt + 1) * 128], r=("sg1",), w=("stg",))
                sch.op("pe", lambda e: e.transpose(S["ptrf"][:, :], S["stg"][:, 0:128], S["identf"][:, :]),
                       r=("stg", "identf"), w=("ptrf",))
                sch.op("ve", lambda e, dm=dm: e.tensor_tensor(
                    S["x1tile"][:, dm * 128:(dm + 1) * 128], S["ptrf"][:, :],
                    S["xtile"][:, dm * 128:(dm + 1) * 128], ALU.add),
                    r=("ptrf", "xtile"), w=(f"x2t.{dm}",))
            sch.dma("po", x2_d[tt * 128:(tt + 1) * 128, :], S["x1tile"][:, :],
                    r=tuple(f"x2t.{dm}" for dm in range(16)), w=("OUT",))

        sch.build(block)
    return nc



_PROGS = {}


def _get_progs():
    if "u1" not in _PROGS:
        _PROGS["u1"] = build_u1(dict(nt_own=4096, ntok=4224, nsel_g=6, nsel_d=2))
        _PROGS["u2"] = build_u2(dict(nu=16, t_seq=8192))
        _PROGS["u3"] = build_u3(dict(nt_own=4096, ntok=4224))
        _PROGS["u4"] = build_u4(dict(nt_own=4096))
        _PROGS["u5"] = build_u5(dict(nt_own=4096))
    return _PROGS


def kernel(**inp):
    P = _get_progs()
    f32 = np.float32
    x = np.ascontiguousarray(inp["L0_attn_hidden_states"], dtype=f32)
    D = 2048
    NT = 4096
    cores = list(range(8))

    identb = np.eye(128).astype(NPBF)
    identf = np.eye(128, dtype=f32)
    onescol = np.ones((128, 1), f32)
    onesrow = np.ones((1, 128), f32)
    triu = np.triu(np.ones((128, 128), f32))
    mup = np.where(np.triu(np.ones((128, 128), bool)), 0, -1e30).astype(f32)
    mlow = np.where(np.tril(np.ones((128, 128), bool)), 0, -1e30).astype(f32)
    ones128 = np.ones((128, 128), f32)
    e127 = np.zeros((128, 1), f32); e127[127] = 1.0

    gt = inp["L0_ffn_gate_idx"].astype(np.int32).reshape(43, 128, D)
    ut = inp["L0_ffn_up_idx"].astype(np.int32).reshape(43, 128, D)
    dn = inp["L0_ffn_down_idx"].astype(np.int32).reshape(16, 128, 5504)
    gt_sh = np.zeros((8, 6, 128, D), np.int32)
    ut_sh = np.zeros((8, 6, 128, D), np.int32)
    for t in range(43):
        gt_sh[t // 6, t % 6] = gt[t]
        ut_sh[t // 6, t % 6] = ut[t]
    dn_sh = dn.reshape(8, 2, 128, 5504)
    im1 = []
    for c in cores:
        sq, hf = c // 2, c % 2
        xo = x[sq, hf * NT:(hf + 1) * NT]
        pref = np.zeros((3, D), f32) if hf == 0 else x[sq, NT - 3:NT]
        xpad = np.concatenate([pref, xo, np.zeros((125, D), f32)], 0)
        im1.append({
            "x": xpad, "w_qkv": inp["L0_attn_w_qkv"], "w_z": inp["L0_attn_w_z"],
            "w_b": inp["L0_attn_w_b"], "w_a": inp["L0_attn_w_a"],
            "conv_w": inp["L0_attn_conv_w"],
            "dt_bias": inp["L0_attn_dt_bias"].reshape(32, 1),
            "A_log": inp["L0_attn_A_log"].reshape(32, 1),
            "lut": inp["L0_ffn_lut"].reshape(1, 16),
            "gidx": gt_sh[c], "uidx": ut_sh[c], "didx": dn_sh[c],
            "identb": identb, "onescol": onescol, "onesrow": onesrow,
        })
    r1 = run_bass_kernel_spmd(P["u1"], im1, core_ids=cores).results

    qf = [np.concatenate([r1[2 * s]["q"], r1[2 * s + 1]["q"]], axis=2) for s in range(4)]
    kf = [np.concatenate([r1[2 * s]["k"], r1[2 * s + 1]["k"]], axis=2) for s in range(4)]
    vf = [np.concatenate([r1[2 * s]["v"], r1[2 * s + 1]["v"]], axis=2) for s in range(4)]
    gvs = [np.concatenate([np.asarray(r1[2 * s]["gv"])[:, 3:3 + NT],
                           np.asarray(r1[2 * s + 1]["gv"])[:, 3:3 + NT]], axis=1) for s in range(4)]
    bvs = [np.concatenate([np.asarray(r1[2 * s]["bv"])[:, 3:3 + NT],
                           np.asarray(r1[2 * s + 1]["bv"])[:, 3:3 + NT]], axis=1) for s in range(4)]
    im2 = []
    for c in cores:
        sq, hf = c // 2, c % 2
        vh0 = 16 * hf
        im2.append({
            "q": np.ascontiguousarray(qf[sq][8 * hf:8 * hf + 8]),
            "k": np.ascontiguousarray(kf[sq][8 * hf:8 * hf + 8]),
            "v": np.ascontiguousarray(vf[sq][vh0:vh0 + 16]),
            "gcols": np.ascontiguousarray(gvs[sq][vh0:vh0 + 16].reshape(16, 64, 128).transpose(0, 2, 1)),
            "bcols": np.ascontiguousarray(bvs[sq][vh0:vh0 + 16].reshape(16, 64, 128).transpose(0, 2, 1)),
            "nwrow": inp["L0_attn_norm_weight"].reshape(1, 128).astype(f32),
            "identb": identb, "identf": identf, "triu": triu, "mup": mup, "mlow": mlow,
            "ones128": ones128, "onesrow": onesrow, "e127": e127,
        })
    r2 = run_bass_kernel_spmd(P["u2"], im2, core_ids=cores).results

    def full_gu(key):
        W = np.zeros((16, 43, 128, 128), NPBF)
        for t in range(43):
            W[:, t] = np.asarray(r1[t // 6][key]).reshape(6, 16, 128, 128)[t % 6]
        return np.ascontiguousarray(W.transpose(0, 2, 1, 3).reshape(2048, 5504))
    wgt_full = full_gu("wgts")
    wut_full = full_gu("wuts")
    Wd = np.zeros((43, 16, 128, 128), NPBF)
    for c in cores:
        sh = np.asarray(r1[c]["wdts"]).reshape(2, 43, 128, 128)
        Wd[:, 2 * c] = sh[0]
        Wd[:, 2 * c + 1] = sh[1]
    wdt_full = np.ascontiguousarray(Wd.transpose(0, 2, 1, 3).reshape(5504, 2048))

    im3 = []
    for c in cores:
        sq, hf = c // 2, c % 2
        vh0 = 16 * hf
        oin = np.zeros((32, 32, 128, 128), NPBF)
        for vh in range(32):
            src_core = 2 * sq + vh // 16
            u = vh % 16
            oin[vh] = np.asarray(r2[src_core]["o"]).reshape(16, 8192, 128)[u, hf * NT:(hf + 1) * NT].reshape(32, 128, 128)
        im3.append({
            "x": x[sq, hf * NT:(hf + 1) * NT], "oin": oin,
            "zs": r1[c]["zs"], "xnT": r1[c]["xnT"],
            "wgt": wgt_full, "wut": wut_full, "wdt": wdt_full,
            "w_out": inp["L0_attn_w_out"],
            "identb": identb, "identf": identf,
        })
    r3 = run_bass_kernel_spmd(P["u3"], im3, core_ids=cores).results

    wqT = np.ascontiguousarray(inp["L1_attn_wq"].T)
    wkT = np.ascontiguousarray(inp["L1_attn_wk"].T)
    wvT = np.ascontiguousarray(inp["L1_attn_wv"].T)
    bqm = np.ascontiguousarray(inp["L1_attn_bq"].reshape(16, 128).T)
    bkm = np.ascontiguousarray(inp["L1_attn_bk"].reshape(16, 128).T)
    bvr = inp["L1_attn_bv"].reshape(1, D)
    wmqT = np.ascontiguousarray(inp["L1_attn_wmap_q"].T)
    wmkT = np.ascontiguousarray(inp["L1_attn_wmap_k"].T)
    im4 = []
    for c in cores:
        im4.append({
            "x1": np.asarray(r3[c]["x1"]).reshape(NT, D),
            "wqT": wqT, "wkT": wkT, "wvT": wvT, "bq": bqm, "bk": bkm, "bvrow": bvr,
            "wmapqT": wmqT, "wmapkT": wmkT,
            "identb": identb, "onesrow": onesrow,
        })
    r4 = run_bass_kernel_spmd(P["u4"], im4, core_ids=cores).results

    woT = np.ascontiguousarray(inp["L1_attn_wo"].T)
    bom = np.ascontiguousarray(inp["L1_attn_bo"].reshape(16, 128).T)
    wfcT = np.ascontiguousarray(inp["L1_ffn_w_fc"].T)
    wpjT = np.ascontiguousarray(inp["L1_ffn_w_proj"].T)
    im5 = []
    for c in cores:
        sq, hf = c // 2, c % 2
        im5.append({
            "x1": np.asarray(r3[c]["x1"]).reshape(NT, D),
            "xn1T": r4[c]["xn1T"], "q2": r4[c]["q2"],
            "kvpa": np.asarray(r4[2 * sq]["kvp"]).reshape(16, 64, 129),
            "kvpb": np.asarray(r4[2 * sq + 1]["kvp"]).reshape(16, 64, 129),
            "woT": woT, "bo": bom, "wfcT": wfcT, "wpjT": wpjT,
            "identb": identb, "identf": identf, "onesrow": onesrow, "onescol": onescol,
        })
    r5 = run_bass_kernel_spmd(P["u5"], im5, core_ids=cores).results

    out = np.empty((4, 8192, D), f32)
    for c in cores:
        sq, hf = c // 2, c % 2
        out[sq, hf * NT:(hf + 1) * NT] = np.asarray(r5[c]["x2"]).reshape(NT, D)
    return out
